# version 10 (final)
# speedup vs baseline: 1.2305x; 1.0906x over previous
import math
from contextlib import ExitStack

import numpy as np
import ml_dtypes

import concourse.bass as bass
import concourse.mybir as mybir
from concourse.bass_utils import run_bass_kernel_spmd

F32 = mybir.dt.float32
BF16 = mybir.dt.bfloat16
AF = mybir.ActivationFunctionType
ALU = mybir.AluOpType
AX = mybir.AxisListType

NCORES = 8
T = 2048
D = 1024
NT = 16
NC = 8
NH = 16
DK = 64
NPAIR = 8
NEG = -30000.0
N_COOR = 50
N_BISECT = 14
BF = ml_dtypes.bfloat16


class _Op:
    __slots__ = ("eng", "fn", "reads", "writes", "dma", "idx", "deps",
                 "published", "sem", "target", "waits", "tag")

    def __init__(self, eng, fn, reads, writes, dma, idx, tag):
        self.eng, self.fn = eng, fn
        self.reads, self.writes = tuple(reads), tuple(writes)
        self.dma, self.idx, self.tag = dma, idx, tag
        self.deps = []
        self.published = False
        self.sem = self.target = self.waits = None


class Prog:
    CENGS = ("tensor", "vector", "scalar", "gpsimd")

    def __init__(self):
        self.ops = []
        self.last_write = {}
        self.readers = {}

    def add(self, eng, fn, reads=(), writes=(), dma=False, tag=""):
        op = _Op(eng, fn, reads, writes, dma, len(self.ops), tag)
        deps = []
        for r in op.reads:
            w = self.last_write.get(r)
            if w is not None:
                deps.append(w)
        for w in op.writes:
            lw = self.last_write.get(w)
            if lw is not None:
                deps.append(lw)
            deps.extend(self.readers.get(w, ()))
        seen = set()
        for d in deps:
            if d.idx in seen or d is op:
                continue
            seen.add(d.idx)
            if d.eng == eng == "tensor":
                continue
            op.deps.append(d)
        for r in op.reads:
            self.readers.setdefault(r, []).append(op)
        for w in op.writes:
            self.last_write[w] = op
            self.readers[w] = []
        self.ops.append(op)
        return op

    def emit(self, nc, es):
        eng_sems = {e: es.enter_context(nc.semaphore(f"se_{e}")) for e in self.CENGS}
        NPOOL = 12
        pools = {"sync": [es.enter_context(nc.semaphore(f"sp_s{i}")) for i in range(NPOOL)],
                 "gpsimd": [es.enter_context(nc.semaphore(f"sp_g{i}")) for i in range(NPOOL)]}
        for op in self.ops:
            groups = {}
            for d in op.deps:
                gk = (d.eng, d.dma)
                cur = groups.get(gk)
                if cur is None or d.idx > cur.idx:
                    groups[gk] = d
            op.deps = list(groups.values())
        for op in self.ops:
            for d in op.deps:
                d.published = True
            if op.dma:
                op.published = True
        out_dmas = [op for op in self.ops if op.dma and op.tag.startswith("out")]
        eng_count = {e: 0 for e in self.CENGS}
        pidx = {k: 0 for k in pools}
        pcount = {}
        for op in self.ops:
            if not op.published:
                continue
            if op.dma:
                q = op.eng if op.eng in pools else "sync"
                s = pools[q][pidx[q] % len(pools[q])]
                pidx[q] += 1
                pcount[id(s)] = pcount.get(id(s), 0) + 16
                op.sem, op.target = s, pcount[id(s)]
            else:
                eng_count[op.eng] += 1
                op.sem, op.target = eng_sems[op.eng], eng_count[op.eng]
        marks = {}
        for op in self.ops:
            want = {}
            for d in op.deps:
                k = id(d.sem)
                if want.get(k, (None, -1))[1] < d.target:
                    want[k] = (d.sem, d.target)
            ww = []
            for k, (sem, tgt) in want.items():
                mk = (op.eng, op.dma, k)
                if marks.get(mk, 0) >= tgt:
                    continue
                marks[mk] = tgt
                ww.append((sem, tgt))
            op.waits = ww
        streams = {}
        for op in self.ops:
            streams.setdefault(op.eng, []).append(op)
        fw = {}
        for op in out_dmas:
            k = id(op.sem)
            if fw.get(k, (None, -1))[1] < op.target:
                fw[k] = (op.sem, op.target)

        def run_stream(eng_obj, ops, is_sync=False):
            for op in ops:
                for sem, tgt in op.waits:
                    eng_obj.wait_ge(sem, tgt)
                inst = op.fn(eng_obj)
                if op.published:
                    inst.then_inc(op.sem, 16 if op.dma else 1)
            if is_sync:
                for sem, tgt in fw.values():
                    eng_obj.wait_ge(sem, tgt)

        with nc.Block() as block:
            for e in ("tensor", "vector", "scalar", "gpsimd"):
                if e in streams:
                    getattr(block, e)(lambda o, e=e: run_stream(o, streams[e]))
            block.sync(lambda o: run_stream(o, streams.get("sync", []), is_sync=True))


def host_constants():
    c = {}
    c["c_idnf"] = np.eye(128, dtype=np.float32)
    c["c_idnb"] = np.eye(128).astype(BF)
    P = np.zeros((128, 128), dtype=np.float32)
    for i in range(64):
        P[2 * i + 1, 2 * i] = 1.0
        P[2 * i, 2 * i + 1] = 1.0
    c["c_perm"] = P.astype(BF)
    c["c_tril"] = (np.arange(128)[:, None] <= np.arange(128)[None, :]).astype(np.float32)
    c["c_mask0"] = np.where(np.arange(128)[None, :] >= np.arange(128)[:, None],
                            0.0, NEG).astype(np.float32)
    c["c_ones"] = np.ones((128, 128), dtype=np.float32)
    c["c_onesb"] = np.ones((1, 512), dtype=BF)
    c["c_frac"] = np.ascontiguousarray(
        np.broadcast_to(((np.arange(7) + 1) / 8.0).astype(np.float32), (128, 7)))
    dd = np.arange(128)
    ii = (dd % 64) // 2
    ang = (1.0 / 10000.0 ** (2.0 * ii / 64.0))[:, None] * np.arange(T)[None, :]
    sgn = np.where(dd % 2 == 0, -1.0, 1.0)[:, None]
    c["c_cos"] = np.cos(ang).astype(BF)
    c["c_sin"] = (np.sin(ang) * sgn).astype(BF)
    m64 = np.where(np.arange(64)[None, :] >= np.arange(64)[:, None],
                   0.0, NEG).astype(np.float32)
    c["c_mask64"] = np.ascontiguousarray(np.tile(m64, (2, 1)))
    rhoc = np.zeros((128, 64), np.float32)
    rhoc[:, 0] = 1.0
    c["c_rhoc"] = rhoc.astype(BF)
    return c


def stage_weights(inp):
    w = {}
    f32 = lambda a: np.ascontiguousarray(np.asarray(a, np.float32))
    w["WqT"] = f32(inp["L0_attn_Wq"].T)
    w["WkT"] = f32(inp["L0_attn_Wk"].T)
    w["WvT"] = f32(inp["L0_attn_Wv"].T)
    w["WoT"] = f32(inp["L0_attn_Wo"].T)
    w["w1lT"] = f32(inp["L0_ffn_w1_l"].T)
    w["w2lT"] = f32(inp["L0_ffn_w2_l"].T)
    w["w1hT"] = f32(inp["L0_ffn_w1_h"].T)
    w["w2hT"] = f32(inp["L0_ffn_w2_h"].T)
    w["wqkv"] = f32(inp["L1_attn_w_qkv"])
    w["wout"] = f32(inp["L1_attn_w_out"])
    w["pmkT"] = f32(np.transpose(np.asarray(inp["L1_attn_pm_k"]), (0, 2, 1)))
    w["pmv"] = f32(inp["L1_attn_pm_v"])
    w["EWq"] = f32(inp["L1_ffn_Wq"])
    w["EWk"] = f32(inp["L1_ffn_Wk"])
    w["EWv"] = f32(inp["L1_ffn_Wv"])
    w["Wg1"] = f32(inp["L1_ffn_Wg1"])
    w["Wg2"] = f32(inp["L1_ffn_Wg2"])
    w["WoF"] = f32(inp["L1_ffn_Wo"])
    col = lambda a, n: np.ascontiguousarray(np.asarray(a, np.float32).reshape(n, 128).T)
    w["b1l"] = col(inp["L0_ffn_b1_l"], 4)
    w["b1h"] = col(inp["L0_ffn_b1_h"], 32)
    w["b2lc"] = col(inp["L0_ffn_b2_l"], 8)
    w["b2h_row"] = f32(inp["L0_ffn_b2_h"])[None, :]
    w["boc"] = col(inp["L1_ffn_bo"], 8)
    w["gl"] = col(inp["L0_ffn_gamma_l"], 8)
    w["gh"] = col(inp["L0_ffn_gamma_h"], 8)
    w["g1a"] = col(inp["L1_attn_gamma"], 8)
    w["bg1"] = col(inp["L1_ffn_bg1"], 1)
    w["bg2_row"] = f32(inp["L1_ffn_bg2"])[None, :]
    bqkvc = np.zeros((128, 96), dtype=np.float32)
    for pi, nm in enumerate(("bq", "bk", "bv")):
        b = f32(inp["L1_ffn_" + nm])
        for e in range(4):
            bqkvc[:, e * 24 + pi * 8:e * 24 + pi * 8 + 8] = b[e].reshape(8, 128).T
    w["bqkvc"] = bqkvc
    w["rt_bc"] = np.ascontiguousarray(
        np.broadcast_to(f32(inp["L0_ffn_routing_token"]), (128, 1024)))
    return w


class Build:
    def __init__(self, nc, es, prog, meta, dbg_names=()):
        self.nc, self.es, self.p = nc, es, prog
        self.dbg_names = set(dbg_names)
        dram = lambda n, shp, dt, kind: nc.dram_tensor(n, shp, dt, kind=kind).ap()
        self.x_in = dram("x_in", [2, T, D], F32, "ExternalInput")
        self.x_out = dram("x_out", [2, T, D], F32, "ExternalOutput")
        self.W = {}
        for name, arr in meta.items():
            dt = BF16 if arr.dtype == BF else F32
            self.W[name] = dram("hb_" + name, list(arr.shape), dt, "ExternalInput")
        self.scr_w1h = dram("scr_w1h", [D, 4096], BF16, "Internal")
        self.scr_w2h = dram("scr_w2h", [4096, D], BF16, "Internal")
        self.scr_qkv = dram("scr_qkv", [D, 3072], BF16, "Internal")
        self.scr_wc = dram("scr_wc", [3, D, D], BF16, "Internal")

        sb = lambda n, shp, dt: es.enter_context(nc.sbuf_tensor(n, shp, dt))
        ps = lambda n, shp: es.enter_context(nc.psum_tensor(n, shp, F32))
        self.x_sb = sb("x_sb", [128, NT * D], F32)
        self.xnT = sb("xnT", [128, NC * T], BF16)
        self.anorm = sb("anorm", [128, NC * T], BF16)
        self.qs = sb("qs", [128, T], BF16)
        self.ks = sb("ks", [128, T], BF16)
        self.vs = sb("vs", [128, 2048], BF16)
        self.ptile = sb("ptile", [128, 2 * 512], BF16)
        self.basep = sb("shb", [128, NC * 512], BF16)
        self.c_cs = self.basep[:, 0:4096]
        self.h1l = self.qs[:, 0:2048]
        self.wsA = sb("wsA", [128, 4096], BF16)
        self.wsB = sb("wsB", [128, 4096], BF16)
        self.wstg = sb("wstg", [128, 1024], F32)
        self.mbc = sb("mbc", [128, T], BF16)
        self.rbc = sb("rbc", [128, 512], F32)
        self.qtmp = sb("qtmp", [128, 512], BF16)
        self.t1b = sb("t1b", [128, 512], BF16)
        self.t2b = sb("t2b", [128, 512], BF16)
        self.tf32 = sb("tf32", [128, 1024], F32)
        self.c_idnf = sb("c_idnf", [128, 128], F32)
        self.c_idnb = sb("c_idnb", [128, 128], BF16)
        self.c_perm = sb("c_perm", [128, 128], BF16)
        self.c_tril = sb("c_tril", [128, 128], F32)
        self.c_mask0 = sb("c_mask0", [128, 128], F32)
        self.c_mask64 = sb("c_mask64", [128, 64], F32)
        self.c_rhoc = sb("c_rhoc", [128, 64], BF16)
        self.c_ones = sb("c_ones", [128, 128], F32)
        self.c_frac = sb("c_frac", [128, 7], F32)
        self.c_rt = sb("c_rt", [128, 1024], F32)
        self.b1l = sb("b1l", [128, 4], F32)
        self.b1h = sb("b1h", [128, 32], F32)
        self.b2lc = sb("b2lc", [128, 8], F32)
        self.boc = sb("boc", [128, 8], F32)
        self.rows_f = sb("rows_f", [1, 256], F32)
        self.rows_b = sb("rows_b", [1, 1536], BF16)
        self.rs_row = sb("rs_row", [1, 2048], F32)
        self.r1 = self.rows_f[0:1, 0:128]
        self.grow = self.rows_f[0:1, 128:192]
        self.one1f = self.rows_f[0:1, 192:193]
        self.bg2_row = self.rows_f[0:1, 200:204]
        self.b2h_row = self.rows_b[0:1, 0:1024]
        self.c_onesb = self.rows_b[0:1, 1024:1536]
        self.gl = sb("gl", [128, 8], F32)
        self.gh = sb("gh", [128, 8], F32)
        self.g1a = sb("g1a", [128, 8], F32)
        self.bg1 = sb("bg1", [128, 1], F32)
        self.wg2 = sb("wg2", [128, 4], F32)
        self.bqkvc = sb("bqkvc", [128, 96], F32)
        self.pmk = sb("pmk", [128, NH * 16], BF16)
        self.pmv = sb("pmv", [128, NH * 64], BF16)
        self.bc_col = sb("bc_col", [128, 24], F32)
        self.st = {n: sb("st_" + n, [128, 16], F32)
                   for n in ("p1", "p2", "ss", "rs", "sB", "u", "sc",
                             "eq", "cs", "m")}
        self.stT = sb("stT", [16, 128], F32)
        self.lo = sb("lo", [128, 1], F32)
        self.hi = sb("hi", [128, 1], F32)
        self.thr = sb("thr", [128, 7], F32)
        self.cnt = sb("cnt", [128, 7], F32)
        self.rtmp = sb("rtmp", [128, 16], F32)
        self.rtm2 = sb("rtm2", [128, 7], F32)
        self.acol = sb("acol", [128, 1], F32)
        self.bcol = sb("bcol", [128, 1], F32)
        self.ccol = sb("ccol", [128, 1], F32)
        self.gw4 = sb("gw4", [4, 1], F32)
        self.gwb = sb("gwb", [128, 4], F32)
        self.g1c = sb("g1c", [128, 1], F32)
        self.poolc = sb("poolc", [128, 8], F32)
        self.pb = [ps(f"pb{i}", [128, 512]) for i in range(8)]
        self.PB = [f"pb{i}" for i in range(8)]

    def add(self, eng, fn, r=(), w=(), dma=False, tag=""):
        return self.p.add(eng, fn, r, w, dma, tag)

    def tap(self, name, ap, reads):
        if name not in self.dbg_names:
            return
        out = self.nc.dram_tensor("dbg_" + name, list(ap.shape), ap.dtype,
                                  kind="ExternalOutput").ap()
        self.add("sync", lambda e, o=out, a=ap: e.dma_start(o, a),
                 r=tuple(reads), dma=True, tag="out_dbg_" + name)

    @staticmethod
    def an_keys(c0, c1):
        ks = [f"an{c}" for c in range(c0 // T, (c1 + T - 1) // T)]
        ks += [f"h1h_{n}" for n in range(c0 // 512, (c1 + 511) // 512)]
        return ks

    def load_consts(self):
        W = self.W
        plain = [(self.c_idnf, "c_idnf", "c_idnf"),
                 (self.c_tril, "c_tril", "c_tril"),
                 (self.c_mask0, "c_mask0", "c_mask0"),
                 (self.c_mask64, "c_mask64", "c_mask64"),
                 (self.c_ones, "c_ones", "c_ones"),
                 (self.c_frac, "c_frac", "c_frac"),
                 (self.c_rt, "rt_bc", "c_rt"),
                 (self.b1l, "b1l", "b1l"), (self.b1h, "b1h", "b1h"),
                 (self.gl, "gl", "gl"), (self.gh, "gh", "gh"),
                 (self.g1a, "g1a", "g1a"),
                 (self.bg1, "bg1", "bg1"),
                 (self.bg2_row, "bg2_row", "bg2_row"),
                 (self.b2lc, "b2lc", "b2lc"), (self.boc, "boc", "boc"),
                 (self.bqkvc, "bqkvc", "bqkvc"),
                 (self.c_idnb, "c_idnb", "c_idnb"),
                 (self.c_rhoc, "c_rhoc", "c_rhoc"),
                 (self.c_perm, "c_perm", "c_perm")]
        for sbuf, wn, key in plain:
            self.add("sync", lambda e, s=sbuf, w=W[wn]: e.dma_start(s[:, :], w),
                     w=[key], dma=True, tag="cl_" + wn)
        self.add("vector", lambda e: e.memset(self.c_onesb[:, :], 1.0),
                 w=["c_onesb"], tag="onesb")
        self.add("sync", lambda e: e.dma_start(self.wg2[0:128, 0:4], W["Wg2"]),
                 w=["wg2"], dma=True, tag="cl_wg2")
        self.add("gpsimd", lambda e: e.dma_start(
            self.b2h_row[0:1, :], W["b2h_row"]),
            w=["b2h_row"], dma=True, tag="clc_b2h")
        for h in range(NH):
            self.add("gpsimd", lambda e, h=h: e.dma_start(
                self.pmk[0:64, h * 16:(h + 1) * 16], W["pmkT"][h]),
                w=["pmk"], dma=True, tag=f"cl_pmk{h}")
            self.add("gpsimd", lambda e, h=h: e.dma_start(
                self.pmk[64:128, h * 16:(h + 1) * 16], W["pmkT"][h]),
                w=["pmk"], dma=True, tag=f"cl_pmk2{h}")
            self.add("gpsimd", lambda e, h=h: e.dma_start(
                self.pmv[0:16, h * 64:(h + 1) * 64], W["pmv"][h]),
                w=["pmv"], dma=True, tag=f"cl_pmv{h}")
            self.add("gpsimd", lambda e, h=h: e.dma_start(
                self.pmv[32:48, h * 64:(h + 1) * 64], W["pmv"][h]),
                w=["pmv"], dma=True, tag=f"cl_pmv2{h}")
        self.add("vector", lambda e: e.memset(self.one1f[:, :], 1.0),
                 w=["one1f"], tag="one1f")
        self.add("vector", lambda e: e.memset(self.ptile[:, :], 0.0),
                 w=["pt0", "pt1"], tag="pt_init")

    def load_x(self, r):
        for t in range(NT):
            self.add("sync", lambda e, t=t, r=r: e.dma_start(
                self.x_sb[:, t * D:(t + 1) * D],
                self.x_in[r, t * 128:(t + 1) * 128, :]),
                w=[f"x{t}"], dma=True, tag=f"xl{r}_{t}")

    def store_x(self, r):
        for t in range(NT):
            self.add("sync", lambda e, t=t, r=r: e.dma_start(
                self.x_out[r, t * 128:(t + 1) * 128, :],
                self.x_sb[:, t * D:(t + 1) * D]),
                r=[f"x{t}"], dma=True, tag=f"out_x{r}_{t}")

    def ln_phase(self, r, layer):
        st = self.st

        def tbuf(t):
            return (self.tf32, "tf32") if t % 2 == 0 else (self.wstg, "wstg")

        def stage_A(t):
            xt = self.x_sb[:, t * D:(t + 1) * D]
            tf, tk = tbuf(t)
            self.add("vector", lambda e, xt=xt, t=t: e.reduce_sum(
                st["p2"][:, t:t + 1], xt, AX.X),
                r=[f"x{t}"], w=[f"p2_{t}"], tag=f"lnS{t}")
            self.add("vector", lambda e, t=t: e.tensor_scalar_mul(
                st["p2"][:, t:t + 1], st["p2"][:, t:t + 1], 1.0 / D),
                r=[f"p2_{t}"], w=[f"p2_{t}"], tag=f"lnM{t}")
            self.add("vector", lambda e, xt=xt, t=t, tf=tf: e.tensor_scalar(
                tf[:, :], xt, st["p2"][:, t:t + 1], None, ALU.subtract),
                r=[f"x{t}", f"p2_{t}"], w=[tk], tag=f"lnT{t}")

        def stage_B(t):
            tf, tk = tbuf(t)
            half = (t % 2) * 1024
            self.add("scalar", lambda e, t=t, tf=tf, half=half: e.activation(
                self.mbc[:, half:half + 1024], tf[:, :], AF.Square,
                accum_out=st["ss"][:, t:t + 1]),
                r=[tk], w=["mbc", f"ss_{t}"], tag=f"lnQ{t}")

        def stage_C(t):
            self.add("vector", lambda e, t=t: e.tensor_scalar(
                st["p1"][:, t:t + 1], st["ss"][:, t:t + 1], 1.0 / D, 1e-5,
                ALU.mult, ALU.add),
                r=[f"ss_{t}"], w=[f"p1_{t}"], tag=f"lnV{t}")

        def stage_D(t):
            self.add("scalar", lambda e, t=t: e.activation(
                st["p1"][:, t:t + 1], st["p1"][:, t:t + 1], AF.Ln),
                r=[f"p1_{t}"], w=[f"p1_{t}"], tag=f"lnL{t}")
            self.add("scalar", lambda e, t=t: e.activation(
                st["p1"][:, t:t + 1], st["p1"][:, t:t + 1], AF.Exp, scale=-0.5),
                r=[f"p1_{t}"], w=[f"p1_{t}"], tag=f"lnR{t}")

        def stage_E(t):
            tf, tk = tbuf(t)
            self.add("vector", lambda e, t=t: e.tensor_scalar(
                st["ss"][:, t:t + 1], st["ss"][:, t:t + 1],
                st["p1"][:, t:t + 1], st["p1"][:, t:t + 1], ALU.mult, ALU.mult),
                r=[f"ss_{t}", f"p1_{t}"], w=[f"ss_{t}"], tag=f"lnN{t}")
            self.add("vector", lambda e, t=t, tf=tf: e.tensor_scalar_mul(
                tf[:, :], tf[:, :], st["p1"][:, t:t + 1]),
                r=[tk, f"p1_{t}"], w=[tk], tag=f"lnB{t}")

        def stage_F(t):
            tf, tk = tbuf(t)
            for c in range(NC):
                bk = c % 2
                self.add("tensor", lambda e, c=c, bk=bk, tf=tf: e.transpose(
                    self.pb[bk][:, 0:128],
                    tf[:, c * 128:(c + 1) * 128],
                    self.c_idnf[:, :]),
                    r=[tk, "c_idnf"], w=[self.PB[bk]], tag=f"lnX{t}_{c}")
                self.add("vector", lambda e, c=c, t=t, bk=bk: e.tensor_copy(
                    self.xnT[:, c * T + t * 128:c * T + (t + 1) * 128],
                    self.pb[bk][:, 0:128]),
                    r=[self.PB[bk]], w=[f"xnT{c}"], tag=f"lnE{t}_{c}")
            if layer == 0:
                pA, pB = (2, 3) if t % 2 == 0 else (4, 5)
                for half in range(2):
                    pp = pA if half == 0 else pB
                    self.add("vector", lambda e, half=half, tf=tf, pp=pp:
                             e.tensor_mul(
                        self.pb[pp][:, :],
                        tf[:, half * 512:(half + 1) * 512],
                        self.c_rt[:, half * 512:(half + 1) * 512]),
                        r=[tk, "c_rt"], w=[self.PB[pp]], tag=f"sM{t}_{half}")
                self.add("vector", lambda e, pA=pA: e.reduce_sum(
                    self.acol[:, :], self.pb[pA][:, :], AX.X),
                    r=[self.PB[pA]], w=["acol"], tag=f"sR{t}a")
                self.add("vector", lambda e, pB=pB: e.reduce_sum(
                    self.bcol[:, :], self.pb[pB][:, :], AX.X),
                    r=[self.PB[pB]], w=["bcol"], tag=f"sR{t}b")
                self.add("vector", lambda e, t=t: e.tensor_add(
                    st["sB"][:, t:t + 1], self.acol[:, :], self.bcol[:, :]),
                    r=["acol", "bcol"], w=[f"sB_{t}", "sB"], tag=f"sF{t}")
            else:
                for half in range(2):
                    self.add("tensor", lambda e, t=t, half=half, tf=tf: e.matmul(
                        self.pb[6 + half][0:1, 0:512], self.c_ones[:, 0:1],
                        tf[:, half * 512:(half + 1) * 512],
                        start=(t == 0), stop=(t == NT - 1),
                        skip_group_check=True),
                        r=[tk, "c_ones"], w=[self.PB[6 + half]],
                        tag=f"pool{t}_{half}")

        stages = [stage_A, stage_B, stage_C, stage_D, stage_E, stage_F]
        for t0 in range(0, NT, 2):
            for sfn in stages:
                sfn(t0)
                sfn(t0 + 1)
        if layer == 0:
            self.add("scalar", lambda e: e.activation(
                st["rs"][:, :], st["ss"][:, :], AF.Ln),
                r=[f"ss_{t}" for t in range(NT)], w=["rs"], tag="rsLn")
            self.add("scalar", lambda e: e.activation(
                st["rs"][:, :], st["rs"][:, :], AF.Exp, scale=0.5),
                r=["rs"], w=["rs"], tag="rsExp")
            self.add("vector", lambda e: e.tensor_scalar_max(
                st["rs"][:, :], st["rs"][:, :], 1e-12),
                r=["rs"], w=["rs"], tag="rsMax")
            self.add("vector", lambda e: e.reciprocal(st["rs"][:, :], st["rs"][:, :]),
                     r=["rs"], w=["rs"], tag="rsRec")
            self.add("vector", lambda e: e.tensor_scalar_mul(
                st["rs"][:, :], st["rs"][:, :], float(math.sqrt(D))),
                r=["rs"], w=["rs"], tag="rsFin")
        else:
            self.add("vector", lambda e: e.tensor_scalar(
                st["rs"][:, :], st["ss"][:, :], 1.0 / D, 1e-6, ALU.mult, ALU.add),
                r=[f"ss_{t}" for t in range(NT)], w=["rs"], tag="rs1V")
            self.add("scalar", lambda e: e.activation(
                st["rs"][:, :], st["rs"][:, :], AF.Ln),
                r=["rs"], w=["rs"], tag="rs1L")
            self.add("scalar", lambda e: e.activation(
                st["rs"][:, :], st["rs"][:, :], AF.Exp, scale=-0.5),
                r=["rs"], w=["rs"], tag="rs1E")
        self.stat_to_row(st["rs"][:, :], ["rs"], self.rs_row, "rs_row")

    def stat_to_row(self, stat_ap, rkeys, row_tile, rowkey):
        self.add("tensor", lambda e: e.transpose(
            self.pb[0][0:16, 0:128], stat_ap, self.c_idnf[:, :]),
            r=list(rkeys) + ["c_idnf"], w=[self.PB[0]], tag=f"s2rT_{rowkey}")
        self.add("vector", lambda e: e.tensor_copy(
            self.stT[0:16, :], self.pb[0][0:16, 0:128]),
            r=[self.PB[0]], w=["stT"], tag=f"s2rE_{rowkey}")
        for t in range(NT):
            self.add("sync", lambda e, t=t: e.dma_start(
                row_tile[0:1, t * 128:(t + 1) * 128], self.stT[t:t + 1, 0:128]),
                r=["stT"], w=[f"{rowkey}_{t}"], dma=True, tag=f"s2rD_{rowkey}{t}")

    def routing_steps(self):
        st = self.st
        logk = float(np.log(np.float32(1152.0)))

        a1 = float(np.float32(np.float32(logk) - np.float32(np.log(np.float32(2048.0)))))

        def coor_iter(i):
            if i == 0:
                self.add("vector", lambda e: e.tensor_scalar_add(
                    st["u"][:, :], st["sB"][:, :], a1),
                    r=["sB"], w=["u"], tag="co0")
                return
            self.add("vector", lambda e: e.tensor_scalar(
                self.rtmp[:, :], st["u"][:, :], -1.0, 0.0, ALU.mult, ALU.min),
                r=["u"], w=["rtmp"], tag=f"coA{i}")
            self.add("vector", lambda e: e.tensor_add(
                self.rtmp[:, :], self.rtmp[:, :], st["sB"][:, :]),
                r=["rtmp", "sB"], w=["rtmp"], tag=f"coB{i}")
            self.add("scalar", lambda e: e.activation(
                self.rtmp[:, :], self.rtmp[:, :], AF.Exp,
                accum_out=self.acol[:, :]),
                r=["rtmp"], w=["rtmp", "acol"], tag=f"coE{i}")
            self.add("tensor", lambda e: e.matmul(
                self.pb[2][0:128, 0:1], self.c_ones[:, :], self.acol[:, :],
                start=True, stop=True, skip_group_check=True),
                r=["acol", "c_ones"], w=[self.PB[2]], tag=f"coP{i}")
            self.add("scalar", lambda e: e.activation(
                self.bcol[:, :], self.pb[2][0:128, 0:1], AF.Ln),
                r=[self.PB[2]], w=["bcol"], tag=f"coL{i}")
            self.add("vector", lambda e: e.tensor_scalar(
                self.bcol[:, :], self.bcol[:, :], -1.0, logk, ALU.mult, ALU.add),
                r=["bcol"], w=["bcol"], tag=f"coA2{i}")
            self.add("vector", lambda e: e.tensor_scalar_add(
                st["u"][:, :], st["sB"][:, :], self.bcol[:, 0:1]),
                r=["sB", "bcol"], w=["u"], tag=f"coU{i}")

        def scores():
            self.add("vector", lambda e: e.tensor_scalar_min(
                st["sc"][:, :], st["u"][:, :], 0.0),
                r=["u"], w=["sc"], tag="scMin")
            self.add("scalar", lambda e: e.activation(
                st["sc"][:, :], st["sc"][:, :], AF.Exp),
                r=["sc"], w=["sc"], tag="scExp")
            self.add("vector", lambda e: e.memset(self.lo[:, :], 0.0),
                     w=["lo"], tag="loI")
            self.add("vector", lambda e: e.memset(self.hi[:, :], 1.1),
                     w=["hi"], tag="hiI")

        def bisect_stage(s):
            self.add("vector", lambda e: e.tensor_sub(
                self.acol[:, :], self.hi[:, :], self.lo[:, :]),
                r=["hi", "lo"], w=["acol"], tag=f"bsD{s}")
            self.add("vector", lambda e: e.tensor_scalar(
                self.thr[:, :], self.c_frac[:, :], self.acol[:, 0:1],
                self.lo[:, 0:1], ALU.mult, ALU.add),
                r=["acol", "lo", "c_frac"], w=["thr"], tag=f"bsT{s}")
            for j in range(7):
                self.add("vector", lambda e, j=j: e.tensor_scalar(
                    self.rtmp[:, :], st["sc"][:, :], self.thr[:, j:j + 1],
                    None, ALU.is_ge),
                    r=["sc", "thr"], w=["rtmp"], tag=f"bsC{s}_{j}")
                self.add("vector", lambda e, j=j: e.reduce_sum(
                    self.cnt[:, j:j + 1], self.rtmp[:, :], AX.X),
                    r=["rtmp"], w=["cnt"], tag=f"bsR{s}_{j}")
            self.add("tensor", lambda e: e.matmul(
                self.pb[2][0:128, 0:7], self.c_ones[:, :], self.cnt[:, 0:7],
                start=True, stop=True, skip_group_check=True),
                r=["cnt", "c_ones"], w=[self.PB[2]], tag=f"bsP{s}")
            self.add("vector", lambda e: e.tensor_scalar(
                self.cnt[:, :], self.pb[2][0:128, 0:7], 1023.5, None, ALU.is_ge),
                r=[self.PB[2]], w=["cnt"], tag=f"bsG{s}")
            self.add("vector", lambda e: e.tensor_mul(
                self.rtm2[:, :], self.thr[:, :], self.cnt[:, :]),
                r=["cnt", "thr"], w=["rtm2"], tag=f"bsL1{s}")
            self.add("vector", lambda e: e.tensor_reduce(
                self.acol[:, :], self.rtm2[:, :], AX.X, ALU.max),
                r=["rtm2"], w=["acol"], tag=f"bsL2{s}")
            self.add("vector", lambda e: e.scalar_tensor_tensor(
                self.rtm2[:, :], self.cnt[:, :], 1e30, self.thr[:, :],
                ALU.mult, ALU.add),
                r=["cnt", "thr"], w=["rtm2"], tag=f"bsH1{s}")
            self.add("vector", lambda e: e.tensor_max(
                self.lo[:, :], self.lo[:, :], self.acol[:, :]),
                r=["acol", "lo"], w=["lo"], tag=f"bsL3{s}")
            self.add("vector", lambda e: e.tensor_reduce(
                self.acol[:, :], self.rtm2[:, :], AX.X, ALU.min),
                r=["rtm2"], w=["acol"], tag=f"bsH2{s}")
            self.add("vector", lambda e: e.tensor_tensor(
                self.hi[:, :], self.hi[:, :], self.acol[:, :], ALU.min),
                r=["acol", "hi"], w=["hi"], tag=f"bsH3{s}")

        def mask_final():
            st_ = st
            self.add("vector", lambda e: e.tensor_scalar(
                st_["u"][:, :], st_["sc"][:, :], self.hi[:, 0:1], None, ALU.is_ge),
                r=["sc", "hi"], w=["u"], tag="mfG")
            self.add("vector", lambda e: e.reduce_sum(
                self.acol[:, :], st_["u"][:, :], AX.X),
                r=["u"], w=["acol"], tag="mfR")
            self.add("tensor", lambda e: e.matmul(
                self.pb[2][0:128, 0:1], self.c_ones[:, :], self.acol[:, :],
                start=True, stop=True, skip_group_check=True),
                r=["acol", "c_ones"], w=[self.PB[2]], tag="mfP")
            self.add("vector", lambda e: e.tensor_scalar(
                self.ccol[:, :], self.pb[2][0:128, 0:1], -1.0, 1024.0,
                ALU.mult, ALU.add),
                r=[self.PB[2]], w=["ccol"], tag="mfL")
            self.add("vector", lambda e: e.tensor_scalar(
                st_["eq"][:, :], st_["sc"][:, :], self.lo[:, 0:1], None, ALU.is_ge),
                r=["sc", "lo"], w=["eq"], tag="mfE1")
            self.add("vector", lambda e: e.tensor_sub(
                st_["eq"][:, :], st_["eq"][:, :], st_["u"][:, :]),
                r=["eq", "u"], w=["eq"], tag="mfE2")
            self.add("tensor", lambda e: e.matmul(
                self.pb[2][0:128, 0:16], self.c_tril[:, :], st_["eq"][:, :],
                start=True, stop=True, skip_group_check=True),
                r=["eq", "c_tril"], w=[self.PB[2]], tag="mfPf")
            self.add("vector", lambda e: e.tensor_copy(
                st_["cs"][:, :], self.pb[2][0:128, 0:16]),
                r=[self.PB[2]], w=["cs"], tag="mfCs")
            self.add("sync", lambda e: e.dma_start(
                self.r1[0:1, 0:16], st_["cs"][127:128, :]),
                r=["cs"], w=["r1"], dma=True, tag="mfCt")
            self.add("vector", lambda e: e.tensor_copy(
                self.r1[0:1, 32:48], self.r1[0:1, 0:16]),
                r=["r1"], w=["r1b"], tag="mfTot")
            cur = 0
            for step in range(4):
                sh = 1 << step
                src, dst = cur, 16 - cur
                self.add("vector", lambda e, sh=sh, src=src, dst=dst: e.tensor_copy(
                    self.r1[0:1, dst:dst + sh], self.r1[0:1, src:src + sh]),
                    r=["r1"], w=["r1"], tag=f"mfHc{step}")
                self.add("vector", lambda e, sh=sh, src=src, dst=dst: e.tensor_add(
                    self.r1[0:1, dst + sh:dst + 16],
                    self.r1[0:1, src + sh:src + 16],
                    self.r1[0:1, src:src + 16 - sh]),
                    r=["r1"], w=["r1"], tag=f"mfHa{step}")
                cur = dst
            self.add("vector", lambda e, cur=cur: e.tensor_sub(
                self.r1[0:1, 48:64], self.r1[0:1, cur:cur + 16],
                self.r1[0:1, 32:48]),
                r=["r1", "r1b"], w=["r1c"], tag="mfEx")
            self.add("tensor", lambda e: e.matmul(
                self.pb[2][0:128, 16:32], self.c_ones[0:1, :],
                self.r1[0:1, 48:64], start=True, stop=True,
                skip_group_check=True),
                r=["r1c", "c_ones"], w=[self.PB[2]], tag="mfOb")
            self.add("vector", lambda e: e.tensor_add(
                st_["cs"][:, :], st_["cs"][:, :], self.pb[2][0:128, 16:32]),
                r=["cs", self.PB[2]], w=["cs"], tag="mfCum")
            self.add("vector", lambda e: e.tensor_scalar(
                self.rtmp[:, :], st_["cs"][:, :], self.ccol[:, 0:1], None,
                ALU.is_le),
                r=["cs", "ccol"], w=["rtmp"], tag="mfLe")
            self.add("vector", lambda e: e.tensor_mul(
                st_["m"][:, :], st_["eq"][:, :], self.rtmp[:, :]),
                r=["rtmp", "eq"], w=["maskB"], tag="mfSe")
            self.add("vector", lambda e: e.tensor_add(
                st_["m"][:, :], st_["m"][:, :], st_["u"][:, :]),
                r=["maskB", "u"], w=["maskB"], tag="mfM")
            self.add("tensor", lambda e: e.transpose(
                self.pb[2][0:16, 0:128], st_["m"][:, :], self.c_idnf[:, :]),
                r=["maskB", "c_idnf"], w=[self.PB[2]], tag="mkT")
            self.add("vector", lambda e: e.tensor_copy(
                self.stT[0:16, :], self.pb[2][0:16, 0:128]),
                r=[self.PB[2]], w=["stT"], tag="mkE")
            for t in range(NT):
                self.add("sync", lambda e, t=t: e.dma_start(
                    self.r1[0:1, 0:128], self.stT[t:t + 1, 0:128]),
                    r=["stT", "r1"], w=["r1"], dma=True, tag=f"mkD{t}")
                self.add("tensor", lambda e, t=t: e.matmul(
                    self.pb[2][:, 0:128], self.c_ones[0:1, 0:128],
                    self.r1[0:1, 0:128],
                    start=True, stop=True, skip_group_check=True),
                    r=["r1", "c_ones"], w=[self.PB[2]], tag=f"mkB{t}")
                self.add("vector", lambda e, t=t: e.tensor_copy(
                    self.mbc[:, t * 128:(t + 1) * 128],
                    self.pb[2][:, 0:128]),
                    r=[self.PB[2]], w=["mbc"], tag=f"mkE{t}")

        steps = [lambda i=i: coor_iter(i) for i in range(N_COOR)]
        steps.append(scores)
        steps += [lambda s=s: bisect_stage(s) for s in range(N_BISECT)]
        steps.append(mask_final)
        return steps

    def load_slab(self, slab_ap, skey, src2d, rows, cols, queue="gpsimd",
                  group=4, tag="wl", extra_reads=()):
        r0, r1 = rows
        c0, c1 = cols
        nch = (r1 - r0) // 128
        ncols = c1 - c0
        for g in range(0, nch, group):
            ge = min(g + group, nch)
            self.add(queue, lambda e, g=g, ge=ge: e.dma_start(
                slab_ap[:, g * ncols:ge * ncols].rearrange(
                    "p (c n) -> p c n", c=ge - g),
                src2d[r0 + g * 128:r0 + ge * 128, c0:c1].rearrange(
                    "(c p) n -> p c n", p=128)),
                r=tuple(extra_reads), w=[skey], dma=True, tag=f"{tag}_{g}")

    def pair_qkv(self, pair, slab, skey, rope, mov_key_fn, mov_ap_fn,
                 tq_blocks=4, v_chunks=NT, bias=None, tagp="qk"):
        for tq in range(tq_blocks):
            for which in (0, 1):
                bk = which
                for c in range(NC):
                    self.add("tensor", lambda e, c=c, tq=tq, which=which,
                             slab=slab, bk=bk: e.matmul(
                        self.pb[bk][:, :],
                        slab[:, which * 1024 + c * 128:which * 1024 + (c + 1) * 128],
                        mov_ap_fn(c, tq * 512, (tq + 1) * 512),
                        start=(c == 0), stop=(c == NC - 1), skip_group_check=True),
                        r=[skey, mov_key_fn(c)], w=[self.PB[bk]],
                        tag=f"{tagp}M{pair}_{which}_{tq}_{c}")
            for which, dst, dkey in ((0, self.qs, "qs"), (1, self.ks, "ks")):
                bk = which
                if rope:
                    tmp = self.qtmp if which == 0 else self.t1b
                    tmpk = "qtmp" if which == 0 else "t1b"
                    pbk = 2 + which
                    self.add("scalar", lambda e, bk=bk, tmp=tmp: e.copy(
                        tmp[:, :], self.pb[bk][:, :]),
                        r=[self.PB[bk]], w=[tmpk], tag=f"{tagp}E{pair}_{which}_{tq}")
                    self.add("tensor", lambda e, tmp=tmp, pbk=pbk: e.matmul(
                        self.pb[pbk][:, :], self.c_perm[:, :], tmp[:, :],
                        start=True, stop=True, skip_group_check=True),
                        r=[tmpk, "c_perm"], w=[self.PB[pbk]],
                        tag=f"{tagp}S{pair}_{which}_{tq}")
                else:
                    bias_ap = None if bias is None else bias(which)
                    if bias_ap is None:
                        self.add("scalar", lambda e, tq=tq, dst=dst, dkey=dkey,
                                 bk=bk: e.copy(
                            dst[:, tq * 512:(tq + 1) * 512], self.pb[bk][:, :]),
                            r=[self.PB[bk]], w=[f"{dkey}{tq}"],
                            tag=f"{tagp}E{pair}_{which}_{tq}")
                    else:
                        ba, bkeys = bias_ap
                        self.add("vector", lambda e, tq=tq, dst=dst,
                                 bk=bk, ba=ba: e.tensor_scalar_add(
                            dst[:, tq * 512:(tq + 1) * 512], self.pb[bk][:, :],
                            ba),
                            r=[self.PB[bk]] + list(bkeys), w=[f"{dkey}{tq}"],
                            tag=f"{tagp}E{pair}_{which}_{tq}")
            if rope:
                for which, dst, dkey in ((0, self.qs, "qs"), (1, self.ks, "ks")):
                    tmp = self.qtmp if which == 0 else self.t1b
                    tmpk = "qtmp" if which == 0 else "t1b"
                    pbk = 2 + which
                    self.add("vector", lambda e, tq=tq, dst=dst, tmp=tmp:
                             e.tensor_mul(
                        dst[:, tq * 512:(tq + 1) * 512], tmp[:, :],
                        self.c_cs[:, tq * 512:(tq + 1) * 512]),
                        r=[tmpk, "shb"], w=[f"{dkey}{tq}"],
                        tag=f"{tagp}C{pair}_{which}_{tq}")
                    self.add("vector", lambda e, tq=tq, tmp=tmp, pbk=pbk:
                             e.tensor_mul(
                        tmp[:, :], self.pb[pbk][:, :],
                        self.c_cs[:, T + tq * 512:T + (tq + 1) * 512]),
                        r=[self.PB[pbk], "shb", tmpk], w=[tmpk],
                        tag=f"{tagp}N{pair}_{which}_{tq}")
                    self.add("vector", lambda e, tq=tq, dst=dst, tmp=tmp,
                             dkey=dkey: e.tensor_add(
                        dst[:, tq * 512:(tq + 1) * 512],
                        dst[:, tq * 512:(tq + 1) * 512], tmp[:, :]),
                        r=[tmpk, f"{dkey}{tq}"], w=[f"{dkey}{tq}"],
                        tag=f"{tagp}F{pair}_{which}_{tq}")
        for tk in range(2 * v_chunks):
            vbk = tk % 2
            for c in range(NC):
                self.add("tensor", lambda e, c=c, tk=tk, slab=slab, vbk=vbk:
                         e.matmul(
                    self.pb[vbk][0:64, 0:64],
                    mov_ap_fn(c, tk * 64, (tk + 1) * 64),
                    slab[:, 2048 + c * 128:2048 + c * 128 + 64],
                    start=(c == 0), stop=False, skip_group_check=True,
                    tile_position=(0, 0)),
                    r=[skey, mov_key_fn(c)], w=[self.PB[vbk]],
                    tag=f"{tagp}V{pair}_{tk}_{c}")
                self.add("tensor", lambda e, c=c, tk=tk, slab=slab, vbk=vbk:
                         e.matmul(
                    self.pb[vbk][64:128, 0:64],
                    mov_ap_fn(c, tk * 64, (tk + 1) * 64),
                    slab[:, 2048 + c * 128 + 64:2048 + (c + 1) * 128],
                    start=(c == 0), stop=False, skip_group_check=True,
                    tile_position=(0, 64)),
                    r=[skey, mov_key_fn(c)], w=[self.PB[vbk]],
                    tag=f"{tagp}U{pair}_{tk}_{c}")
            self.add("vector", lambda e, tk=tk, vbk=vbk: e.tensor_copy(
                self.vs[:, tk * 64:(tk + 1) * 64],
                self.pb[vbk][:, 0:64]),
                r=[self.PB[vbk]], w=[f"vs{tk}"], tag=f"{tagp}W{pair}_{tk}")

    def pair_attention(self, pair, mode, window=None, vbias=None,
                       prev_tail=None, blk0=0):
        tq_blocks = 1 if mode == "win" else 4
        tails = [prev_tail] if prev_tail is not None else []
        blk = blk0
        for b in range(tq_blocks):
            pvbk = 4 + (blk % 2)
            dbk = 6 + (blk % 2)
            first = [True]
            frho = [True]
            chunks = []

            def mk_chunk(c64, dcol, masked, tag, pvbk=pvbk, dbk=dbk, b=b):
                sbk = 2 + (c64 % 2)
                psl = c64 % 2
                pt = self.ptile[:, psl * 512:(psl + 1) * 512]

                def emit_S():
                    for hh in range(2):
                        self.add("tensor", lambda e, c64=c64, dcol=dcol, hh=hh,
                                 sbk=sbk, b=b: e.matmul(
                            self.pb[sbk][hh * 64:(hh + 1) * 64, dcol:512],
                            self.ks[hh * 64:(hh + 1) * 64,
                                    c64 * 64:(c64 + 1) * 64],
                            self.qs[hh * 64:(hh + 1) * 64,
                                    b * 512 + dcol:(b + 1) * 512],
                            start=True, stop=True, skip_group_check=True,
                            tile_position=(hh * 64, hh * 64)),
                            r=[f"ks{c64 // 8}", f"qs{b}"], w=[self.PB[sbk]],
                            tag=f"sM{tag}_{hh}")
                    if masked:
                        self.add("vector", lambda e, dcol=dcol, sbk=sbk:
                                 e.tensor_add(
                            self.pb[sbk][:, dcol:dcol + 64],
                            self.pb[sbk][:, dcol:dcol + 64],
                            self.c_mask64[:, :]),
                            r=[self.PB[sbk], "c_mask64"], w=[self.PB[sbk]],
                            tag=f"sK{tag}")

                def emit_EP():
                    if dcol:
                        self.add("vector", lambda e, dcol=dcol, pt=pt:
                                 e.memset(pt[:, 0:dcol], 0.0),
                            w=[f"pt{psl}"], tag=f"sZ{tag}")
                    self.add("scalar", lambda e, dcol=dcol, sbk=sbk, pt=pt:
                             e.activation(
                        pt[:, dcol:512], self.pb[sbk][:, dcol:512],
                        AF.Exp, scale=0.125),
                        r=[self.PB[sbk]], w=[f"pt{psl}"], tag=f"sE{tag}")
                    for hh in range(2):
                        self.add("tensor", lambda e, c64=c64, pt=pt, hh=hh,
                                 f=first[0], pvbk=pvbk: e.matmul(
                            self.pb[pvbk][hh * 64:(hh + 1) * 64, :],
                            self.vs[hh * 64:(hh + 1) * 64,
                                    c64 * 64:(c64 + 1) * 64],
                            pt[hh * 64:(hh + 1) * 64, :],
                            start=f, stop=False, skip_group_check=True,
                            tile_position=(hh * 64, hh * 64)),
                            r=[f"pt{psl}", f"vs{c64}"], w=[self.PB[pvbk]],
                            tag=f"pv{tag}_{hh}")
                    self.add("tensor", lambda e, pt=pt, f=frho[0], dbk=dbk:
                             e.matmul(
                        self.pb[dbk][64:128, :], self.c_rhoc[0:64, :],
                        pt[0:64, :], start=f, stop=False,
                        skip_group_check=True, tile_position=(0, 64)),
                        r=[f"pt{psl}", "c_rhoc"], w=[self.PB[dbk]],
                        tag=f"rh{tag}_0")
                    self.add("tensor", lambda e, pt=pt, f=frho[0], dbk=dbk:
                             e.matmul(
                        self.pb[dbk][0:64, :], self.c_rhoc[64:128, :],
                        pt[64:128, :], start=f, stop=False,
                        skip_group_check=True, tile_position=(64, 0)),
                        r=[f"pt{psl}", "c_rhoc"], w=[self.PB[dbk]],
                        tag=f"rh{tag}_1")
                    first[0] = False
                    frho[0] = False

                return emit_S, emit_EP

            def mk_mem(tag, pvbk=pvbk, dbk=dbk, b=b):
                def emit_S():
                    for hh, po in ((0, 0), (1, 32)):
                        hd = pair * 2 + hh
                        self.add("tensor", lambda e, hh=hh, po=po, hd=hd:
                                 e.matmul(
                            self.pb[3][po:po + 16, :],
                            self.pmk[hh * 64:(hh + 1) * 64,
                                     hd * 16:(hd + 1) * 16],
                            self.qs[hh * 64:(hh + 1) * 64,
                                    b * 512:(b + 1) * 512],
                            start=True, stop=True, skip_group_check=True,
                            tile_position=(hh * 64, po)),
                            r=["pmk", f"qs{b}"], w=[self.PB[3]],
                            tag=f"smM{tag}_{hh}")

                def emit_EP():
                    pt = self.ptile[:, 512:1024]
                    for po in (0, 32):
                        self.add("scalar", lambda e, po=po, pt=pt: e.activation(
                            pt[po:po + 16, :], self.pb[3][po:po + 16, :],
                            AF.Exp, scale=0.125),
                            r=[self.PB[3]], w=["pt1"], tag=f"smE{tag}_{po}")
                    for hh, po in ((0, 0), (1, 32)):
                        hd = pair * 2 + hh
                        self.add("tensor", lambda e, hh=hh, po=po, hd=hd,
                                 pt=pt, pvbk=pvbk: e.matmul(
                            self.pb[pvbk][hh * 64:(hh + 1) * 64, :],
                            self.pmv[po:po + 16, hd * 64:(hd + 1) * 64],
                            pt[po:po + 16, :],
                            start=True, stop=False, skip_group_check=True,
                            tile_position=(po, hh * 64)),
                            r=["pt1", "pmv"], w=[self.PB[pvbk]],
                            tag=f"smP{tag}_{hh}")
                    self.add("tensor", lambda e, pt=pt, dbk=dbk: e.matmul(
                        self.pb[dbk][64:128, :], self.c_rhoc[0:16, :],
                        pt[0:16, :], start=True, stop=False,
                        skip_group_check=True, tile_position=(0, 64)),
                        r=["pt1", "c_rhoc"], w=[self.PB[dbk]],
                        tag=f"smR{tag}_0")
                    self.add("tensor", lambda e, pt=pt, dbk=dbk: e.matmul(
                        self.pb[dbk][0:64, :], self.c_rhoc[32:48, :],
                        pt[32:48, :], start=True, stop=False,
                        skip_group_check=True, tile_position=(32, 0)),
                        r=["pt1", "c_rhoc"], w=[self.PB[dbk]],
                        tag=f"smR{tag}_1")
                    first[0] = False
                    frho[0] = False

                return emit_S, emit_EP

            tg = f"{pair}_{b}"
            if mode == "win":
                chunks.append(mk_mem(tg))
                for c in range(8):
                    chunks.append(mk_chunk(c, 64 * c, True, f"{tg}_{c}"))
            elif mode == "causal":
                for c in range(8 * b):
                    chunks.append(mk_chunk(c, 0, False, f"{tg}_{c}"))
                for i in range(8):
                    chunks.append(mk_chunk(8 * b + i, 64 * i, True,
                                           f"{tg}_d{i}"))
            else:
                for c in range(32):
                    chunks.append(mk_chunk(c, 0, False, f"{tg}_{c}"))
            chunks[0][0]()
            if tails:
                tails.pop(0)()
            for i in range(len(chunks)):
                if i + 1 < len(chunks):
                    chunks[i + 1][0]()
                chunks[i][1]()

            def tail(pvbk=pvbk, dbk=dbk, b=b, tg=tg, pair=pair, mode=mode,
                     window=window, vbias=vbias):
                self.add("scalar", lambda e, dbk=dbk: e.copy(
                    self.rbc[:, :], self.pb[dbk][:, :]),
                    r=[self.PB[dbk]], w=["rbcr", "rbcb"], tag=f"rhC{tg}")
                self.add("tensor", lambda e: e.matmul(
                    self.pb[0][0:64, :], self.c_ones[64:65, 0:64],
                    self.rbc[64:65, 0:512], start=True, stop=True,
                    skip_group_check=True, tile_position=(64, 0)),
                    r=["rbcr", "c_ones"], w=[self.PB[0]], tag=f"rhB{tg}")
                self.add("tensor", lambda e: e.matmul(
                    self.pb[0][64:128, :], self.c_ones[0:1, 0:64],
                    self.rbc[0:1, 0:512], start=True, stop=True,
                    skip_group_check=True, tile_position=(0, 64)),
                    r=["rbcr", "c_ones"], w=[self.PB[0]], tag=f"rhB2{tg}")
                self.add("vector", lambda e: e.reciprocal(
                    self.rbc[:, :], self.pb[0][:, :]),
                    r=[self.PB[0], "rbcr"], w=["rbcb", "rbcr"], tag=f"rhR{tg}")
                tok0 = (window * 512) if mode == "win" else (b * 512)
                col0 = pair * T + tok0
                wk = self.an_keys(col0, col0 + 512)
                self.add("vector", lambda e, col0=col0, pvbk=pvbk: e.tensor_mul(
                    self.anorm[:, col0:col0 + 512], self.pb[pvbk][:, :],
                    self.rbc[:, :]),
                    r=[self.PB[pvbk], "rbcb"], w=wk, tag=f"rhD{tg}")
                if vbias is not None:
                    ba, bkeys = vbias(0)
                    self.add("vector", lambda e, col0=col0, ba=ba:
                             e.tensor_scalar_add(
                        self.anorm[:, col0:col0 + 512],
                        self.anorm[:, col0:col0 + 512], ba),
                        r=wk + list(bkeys), w=wk, tag=f"rhE{tg}")

            tails.append(tail)
            blk += 1
        return tails.pop(0) if tails else None

    def merge_feat_to_x(self, jc, tq0, fbk, tag, bias_col=None, bias_keys=()):
        if bias_col is None:
            self.add("scalar", lambda e: e.copy(self.rbc[:, :],
                                                self.pb[fbk][:, :]),
                     r=[self.PB[fbk]], w=["rbcb", "rbcr"], tag=f"{tag}_ev")
        else:
            self.add("vector", lambda e, bias_col=bias_col: e.tensor_scalar_add(
                self.rbc[:, :], self.pb[fbk][:, :], bias_col),
                r=[self.PB[fbk]] + list(bias_keys), w=["rbcb", "rbcr"],
                tag=f"{tag}_ev")
        for s4 in range(4):
            t16 = tq0 // 128 + s4
            tbk = 6 if s4 % 2 == 0 else 7
            self.add("tensor", lambda e, s4=s4, tbk=tbk: e.transpose(
                self.pb[tbk][:, 128:256], self.rbc[:, s4 * 128:(s4 + 1) * 128],
                self.c_idnf[:, :]),
                r=["rbcb", "rbcr", "c_idnf"], w=[self.PB[tbk]],
                tag=f"{tag}_tr{s4}")
            self.add("vector", lambda e, t16=t16, jc=jc, tbk=tbk: e.tensor_add(
                self.x_sb[:, t16 * D + jc * 128:t16 * D + (jc + 1) * 128],
                self.x_sb[:, t16 * D + jc * 128:t16 * D + (jc + 1) * 128],
                self.pb[tbk][:, 128:256]),
                r=[self.PB[tbk], f"x{t16}"], w=[f"x{t16}"], tag=f"{tag}_ad{s4}")

    def merge_attn(self, layer, wname, bias_ct=None, bias_ck=None):
        W = self.W[wname]
        self.load_slab(self.wsA[:, 0:1024], "wsA", W, (0, D), (0, 128),
                       tag=f"am{layer}_0")
        for jc in range(NC):
            slab = self.wsA if jc % 2 == 0 else self.wsB
            skey = "wsA" if jc % 2 == 0 else "wsB"
            if jc + 1 < NC:
                nslab = self.wsB if jc % 2 == 0 else self.wsA
                nskey = "wsB" if jc % 2 == 0 else "wsA"
                self.load_slab(nslab[:, 0:1024], nskey, W, (0, D),
                               ((jc + 1) * 128, (jc + 2) * 128),
                               tag=f"am{layer}_{jc + 1}")
            for b in range(4):
                fbk = 2 if b % 2 == 0 else 3
                for c in range(NC):
                    self.add("tensor", lambda e, c=c, b=b, fbk=fbk, slab=slab:
                             e.matmul(
                        self.pb[fbk][:, :], slab[:, c * 128:(c + 1) * 128],
                        self.anorm[:, c * T + b * 512:c * T + (b + 1) * 512],
                        start=(c == 0), stop=False, skip_group_check=True),
                        r=[skey] + self.an_keys(c * T + b * 512,
                                                c * T + (b + 1) * 512),
                        w=[self.PB[fbk]], tag=f"amM{layer}_{jc}_{b}_{c}")
                bc = None if bias_ct is None else bias_ct[:, jc:jc + 1]
                bk = () if bias_ct is None else (bias_ck,)
                self.merge_feat_to_x(jc, b * 512, fbk, f"am{layer}_{jc}_{b}",
                                     bias_col=bc, bias_keys=bk)

    def load_pair_w_l0(self, pair):
        slab = self.wsA if pair % 2 == 0 else self.wsB
        skey = "wsA" if pair % 2 == 0 else "wsB"
        for i, wn in enumerate(("WqT", "WkT", "WvT")):
            self.load_slab(slab[:, i * 1024:(i + 1) * 1024], skey,
                           self.W[wn], (0, D),
                           (pair * 128, (pair + 1) * 128),
                           tag=f"l0w{pair}_{i}")

    def attn_l0(self, steps):
        self.add("sync", lambda e: e.dma_start(
            self.c_cs[:, 0:T], self.W["c_cos"]),
            w=["shb"], dma=True, tag="cl_cos")
        self.add("sync", lambda e: e.dma_start(
            self.c_cs[:, T:2 * T], self.W["c_sin"]),
            w=["shb"], dma=True, tag="cl_sin")
        self.load_pair_w_l0(0)
        tail = None
        for pair in range(NPAIR):
            if pair + 1 < NPAIR:
                self.load_pair_w_l0(pair + 1)
            slab = self.wsA if pair % 2 == 0 else self.wsB
            skey = "wsA" if pair % 2 == 0 else "wsB"
            self.pair_qkv(pair, slab, skey, rope=True,
                          mov_key_fn=lambda c: f"xnT{c}",
                          mov_ap_fn=lambda c, lo, hi: self.xnT[:, c * T + lo:c * T + hi],
                          tagp="l0q")
            tail = self.pair_attention(pair, "causal", prev_tail=tail,
                                       blk0=pair * 8)
            for _ in range(9):
                if steps:
                    steps.pop(0)()
        if tail is not None:
            tail()
        while steps:
            steps.pop(0)()

    def ffn_l0(self, r):
        W = self.W
        for tb in range(4):
            for gi, (gt, gkey) in enumerate(((self.gl, "gl"), (self.gh, "gh"))):
                for s4 in range(4):
                    t16 = tb * 4 + s4
                    self.add("tensor", lambda e, t16=t16: e.matmul(
                        self.pb[7][:, 0:128], self.c_ones[0:1, :],
                        self.rs_row[0:1, t16 * 128:(t16 + 1) * 128],
                        start=True, stop=True, skip_group_check=True),
                        r=[f"rs_row_{t16}", "c_ones"], w=[self.PB[7]],
                        tag=f"fbB{tb}_{gi}_{s4}")
                    self.add("scalar", lambda e: e.copy(
                        self.rbc[:, 0:128], self.pb[7][:, 0:128]),
                        r=[self.PB[7]], w=["rbcb"], tag=f"fbE{tb}_{gi}_{s4}")
                    for c in range(NC):
                        self.add("vector", lambda e, c=c, s4=s4, t16=t16, gt=gt:
                                 e.scalar_tensor_tensor(
                            self.basep[:, c * 512 + s4 * 128:c * 512 + (s4 + 1) * 128],
                            self.xnT[:, c * T + t16 * 128:c * T + (t16 + 1) * 128],
                            gt[:, c:c + 1], self.rbc[:, 0:128],
                            ALU.mult, ALU.mult),
                            r=["rbcb", f"xnT{c}", gkey], w=["shb"],
                            tag=f"fbS{tb}_{gi}_{s4}_{c}")
                if gi == 0:
                    self.load_slab(self.wsA[:, 0:4096], "wsA", W["w1lT"],
                                   (0, D), (0, 512), tag=f"l1w{tb}")
                    for n in range(4):
                        hbk = n % 2
                        for c in range(NC):
                            self.add("tensor", lambda e, c=c, n=n, hbk=hbk:
                                     e.matmul(
                                self.pb[hbk][:, :],
                                self.wsA[:, c * 512 + n * 128:c * 512 + (n + 1) * 128],
                                self.basep[:, c * 512:(c + 1) * 512],
                                start=(c == 0), stop=(c == NC - 1),
                                skip_group_check=True),
                                r=["wsA", "shb"], w=[self.PB[hbk]],
                                tag=f"hlM{tb}_{n}_{c}")
                        self.add("scalar", lambda e, n=n, hbk=hbk: e.activation(
                            self.h1l[:, n * 512:(n + 1) * 512],
                            self.pb[hbk][:, :], AF.Gelu,
                            bias=self.b1l[:, n:n + 1]),
                            r=[self.PB[hbk], "b1l"], w=[f"qs{n}"],
                            tag=f"hlG{tb}_{n}")
                else:
                    self.stream_heavy_w1(r, tb, 0, self.wsA, "wsA")
                    for nb in range(8):
                        slab = self.wsA if nb % 2 == 0 else self.wsB
                        skey = "wsA" if nb % 2 == 0 else "wsB"
                        if nb + 1 < 8:
                            self.stream_heavy_w1(
                                r, tb, nb + 1,
                                self.wsB if nb % 2 == 0 else self.wsA,
                                "wsB" if nb % 2 == 0 else "wsA")
                        for nn in range(4):
                            n = nb * 4 + nn
                            hbk = n % 2
                            for c in range(NC):
                                self.add("tensor", lambda e, c=c, nn=nn, hbk=hbk,
                                         slab=slab: e.matmul(
                                    self.pb[hbk][:, :],
                                    slab[:, c * 512 + nn * 128:c * 512 + (nn + 1) * 128],
                                    self.basep[:, c * 512:(c + 1) * 512],
                                    start=(c == 0), stop=(c == NC - 1),
                                    skip_group_check=True),
                                    r=[skey, "shb"], w=[self.PB[hbk]],
                                    tag=f"hhM{tb}_{n}_{c}")
                            self.add("scalar", lambda e, n=n, hbk=hbk: e.activation(
                                self.qtmp[:, :], self.pb[hbk][:, :], AF.Gelu,
                                bias=self.b1h[:, n:n + 1]),
                                r=[self.PB[hbk], "b1h"], w=["qtmp"],
                                tag=f"hhG{tb}_{n}")
                            self.add("vector", lambda e, n=n, tb=tb: e.tensor_mul(
                                self.anorm[:, n * 512:(n + 1) * 512],
                                self.qtmp[:, :],
                                self.mbc[:, tb * 512:(tb + 1) * 512]),
                                r=["qtmp", "mbc"],
                                w=self.an_keys(n * 512, (n + 1) * 512),
                                tag=f"hhK{tb}_{n}")
            self.stream_heavy_w2(r, tb, 0, self.wsA, "wsA")
            for jc in range(NC):
                slab = self.wsA if jc % 2 == 0 else self.wsB
                skey = "wsA" if jc % 2 == 0 else "wsB"
                if jc + 1 < NC:
                    self.stream_heavy_w2(
                        r, tb, jc + 1,
                        self.wsB if jc % 2 == 0 else self.wsA,
                        "wsB" if jc % 2 == 0 else "wsA")
                fbk = 2 if jc % 2 == 0 else 3
                self.add("gpsimd", lambda e, jc=jc: e.dma_start(
                    self.ptile[:, 0:512].rearrange("p (c n) -> p c n", c=4),
                    W["w2lT"][0:512, jc * 128:(jc + 1) * 128].rearrange(
                        "(c p) n -> p c n", p=128)),
                    w=["pt0"], dma=True, tag=f"wl2_{tb}_{jc}")
                for n in range(4):
                    self.add("tensor", lambda e, n=n, jc=jc, fbk=fbk: e.matmul(
                        self.pb[fbk][:, :],
                        self.ptile[:, n * 128:(n + 1) * 128],
                        self.h1l[:, n * 512:(n + 1) * 512],
                        start=(n == 0), stop=False, skip_group_check=True),
                        r=["pt0", f"qs{n}"], w=[self.PB[fbk]],
                        tag=f"foL{tb}_{jc}_{n}")
                for n in range(32):
                    self.add("tensor", lambda e, n=n, fbk=fbk, slab=slab: e.matmul(
                        self.pb[fbk][:, :], slab[:, n * 128:(n + 1) * 128],
                        self.anorm[:, n * 512:(n + 1) * 512],
                        start=False, stop=False, skip_group_check=True),
                        r=[skey, f"h1h_{n}"], w=[self.PB[fbk]],
                        tag=f"foH{tb}_{jc}_{n}")
                self.add("tensor", lambda e, jc=jc, fbk=fbk, tb=tb: e.matmul(
                    self.pb[fbk][:, :],
                    self.b2h_row[0:1, jc * 128:(jc + 1) * 128],
                    self.mbc[0:1, tb * 512:(tb + 1) * 512], start=False,
                    stop=False, skip_group_check=True),
                    r=["b2h_row", "mbc"], w=[self.PB[fbk]],
                    tag=f"foB2{tb}_{jc}")
                self.merge_feat_to_x(jc, tb * 512, fbk, f"fo{tb}_{jc}",
                                     bias_col=self.b2lc[:, jc:jc + 1],
                                     bias_keys=("b2lc",))

    def stream_heavy_w1(self, r, tb, nb, slab, skey):
        first = (r == 0 and tb == 0)
        src = self.W["w1hT"] if first else self.scr_w1h
        q = "gpsimd"
        for g in range(0, NC, 2):
            self.add(q, lambda e, g=g, nb=nb, slab=slab, src=src: e.dma_start(
                slab[:, g * 512:(g + 2) * 512].rearrange("p (c n) -> p c n", c=2),
                src[g * 128:(g + 2) * 128, nb * 512:(nb + 1) * 512].rearrange(
                    "(c p) n -> p c n", p=128)),
                w=[skey], dma=True, tag=f"hw1_{tb}_{nb}_{g}")
        if first:
            for g in range(0, NC, 2):
                self.add("sync", lambda e, g=g, nb=nb, slab=slab: e.dma_start(
                    self.scr_w1h[g * 128:(g + 2) * 128,
                                 nb * 512:(nb + 1) * 512].rearrange(
                        "(c p) n -> p c n", p=128),
                    slab[:, g * 512:(g + 2) * 512].rearrange(
                        "p (c n) -> p c n", c=2)),
                    r=[skey], w=["scrw1"], dma=True, tag=f"hw1s_{nb}_{g}")

    def stream_heavy_w2(self, r, tb, jc, slab, skey):
        first = (r == 0 and tb == 0)
        src = self.W["w2hT"] if first else self.scr_w2h
        q = "gpsimd" if first else "sync"
        for g in range(0, 32, 8):
            self.add(q, lambda e, g=g, jc=jc, slab=slab, src=src: e.dma_start(
                slab[:, g * 128:(g + 8) * 128].rearrange("p (c n) -> p c n", c=8),
                src[g * 128:(g + 8) * 128, jc * 128:(jc + 1) * 128].rearrange(
                    "(c p) n -> p c n", p=128)),
                w=[skey], dma=True, tag=f"hw2_{tb}_{jc}_{g}")
        if first:
            for g in range(0, 32, 8):
                self.add("sync", lambda e, g=g, jc=jc, slab=slab: e.dma_start(
                    self.scr_w2h[g * 128:(g + 8) * 128,
                                 jc * 128:(jc + 1) * 128].rearrange(
                        "(c p) n -> p c n", p=128),
                    slab[:, g * 128:(g + 8) * 128].rearrange(
                        "p (c n) -> p c n", c=8)),
                    r=[skey], w=["scrw2"], dma=True, tag=f"hw2s_{jc}_{g}")

    def gate_l1(self):
        for c in range(NC):
            bank = 6 if c < 4 else 7
            off = (c % 4) * 128
            self.add("scalar", lambda e, bank=bank, off=off: e.copy(
                self.r1[0:1, 0:128], self.pb[bank][0:1, off:off + 128]),
                r=[self.PB[bank], "r1"], w=["r1"], tag=f"gtE{c}")
            self.add("tensor", lambda e, c=c: e.matmul(
                self.pb[0][0:128, 0:1], self.r1[0:1, 0:128],
                self.one1f[0:1, 0:1], start=True, stop=True,
                skip_group_check=True),
                r=["r1", "one1f"], w=[self.PB[0]], tag=f"gtT{c}")
            self.add("vector", lambda e, c=c: e.tensor_scalar_mul(
                self.poolc[:, c:c + 1], self.pb[0][0:128, 0:1], 1.0 / T),
                r=[self.PB[0]], w=["poolc"], tag=f"gtC{c}")
        for c in range(NC):
            self.add("sync", lambda e, c=c: e.dma_start(
                self.wstg[:, 0:128],
                self.W["Wg1"][c * 128:(c + 1) * 128, :]),
                w=["wstg"], dma=True, tag=f"gtW{c}")
            self.add("tensor", lambda e, c=c: e.matmul(
                self.pb[0][0:128, 0:1],
                self.wstg[:, 0:128],
                self.poolc[:, c:c + 1],
                start=(c == 0), stop=(c == NC - 1), skip_group_check=True),
                r=["wstg", "poolc"], w=[self.PB[0]], tag=f"gtM{c}")
        self.add("scalar", lambda e: e.activation(
            self.g1c[:, :], self.pb[0][0:128, 0:1], AF.Relu,
            bias=self.bg1[:, 0:1]),
            r=[self.PB[0], "bg1"], w=["g1c"], tag="gtR")
        self.add("tensor", lambda e: e.matmul(
            self.pb[0][0:1, 0:4], self.g1c[:, 0:1], self.wg2[0:128, 0:4],
            start=True, stop=True, skip_group_check=True),
            r=["g1c", "wg2"], w=[self.PB[0]], tag="gtL")
        G = self.grow
        self.add("vector", lambda e: e.tensor_add(
            G[0:1, 0:4], self.pb[0][0:1, 0:4], self.bg2_row[0:1, 0:4]),
            r=[self.PB[0], "bg2_row"], w=["grow"], tag="gtLb")
        self.add("vector", lambda e: e.tensor_reduce(
            G[0:1, 16:17], G[0:1, 0:4], AX.X, ALU.max),
            r=["grow"], w=["grow"], tag="gtm1")
        self.add("vector", lambda e: e.tensor_scalar(
            G[0:1, 4:8], G[0:1, 0:4], G[0:1, 16:17], None, ALU.is_ge),
            r=["grow"], w=["grow"], tag="gte1")
        self.add("vector", lambda e: e.tensor_copy(G[0:1, 20:21], G[0:1, 4:5]),
                 r=["grow"], w=["grow"], tag="gtp0")
        self.add("vector", lambda e: e.tensor_add(
            G[0:1, 21:24], G[0:1, 5:8], G[0:1, 4:7]),
            r=["grow"], w=["grow"], tag="gtp1")
        self.add("vector", lambda e: e.tensor_copy(G[0:1, 24:26], G[0:1, 20:22]),
                 r=["grow"], w=["grow"], tag="gtp2")
        self.add("vector", lambda e: e.tensor_add(
            G[0:1, 26:28], G[0:1, 22:24], G[0:1, 20:22]),
            r=["grow"], w=["grow"], tag="gtp3")
        self.add("vector", lambda e: e.tensor_scalar(
            G[0:1, 28:32], G[0:1, 24:28], 1.5, None, ALU.is_le),
            r=["grow"], w=["grow"], tag="gtp4")
        self.add("vector", lambda e: e.tensor_mul(
            G[0:1, 4:8], G[0:1, 4:8], G[0:1, 28:32]),
            r=["grow"], w=["grow"], tag="gts1")
        self.add("vector", lambda e: e.scalar_tensor_tensor(
            G[0:1, 32:36], G[0:1, 4:8], -1e30, G[0:1, 0:4], ALU.mult, ALU.add),
            r=["grow"], w=["grow"], tag="gtl2")
        self.add("vector", lambda e: e.tensor_reduce(
            G[0:1, 17:18], G[0:1, 32:36], AX.X, ALU.max),
            r=["grow"], w=["grow"], tag="gtm2")
        self.add("vector", lambda e: e.tensor_scalar(
            G[0:1, 36:40], G[0:1, 32:36], G[0:1, 17:18], None, ALU.is_ge),
            r=["grow"], w=["grow"], tag="gte2")
        self.add("vector", lambda e: e.tensor_copy(G[0:1, 20:21], G[0:1, 36:37]),
                 r=["grow"], w=["grow"], tag="gtq0")
        self.add("vector", lambda e: e.tensor_add(
            G[0:1, 21:24], G[0:1, 37:40], G[0:1, 36:39]),
            r=["grow"], w=["grow"], tag="gtq1")
        self.add("vector", lambda e: e.tensor_copy(G[0:1, 24:26], G[0:1, 20:22]),
                 r=["grow"], w=["grow"], tag="gtq2")
        self.add("vector", lambda e: e.tensor_add(
            G[0:1, 26:28], G[0:1, 22:24], G[0:1, 20:22]),
            r=["grow"], w=["grow"], tag="gtq3")
        self.add("vector", lambda e: e.tensor_scalar(
            G[0:1, 28:32], G[0:1, 24:28], 1.5, None, ALU.is_le),
            r=["grow"], w=["grow"], tag="gtq4")
        self.add("vector", lambda e: e.tensor_mul(
            G[0:1, 36:40], G[0:1, 36:40], G[0:1, 28:32]),
            r=["grow"], w=["grow"], tag="gts2")
        self.add("vector", lambda e: e.tensor_sub(
            G[0:1, 18:19], G[0:1, 17:18], G[0:1, 16:17]),
            r=["grow"], w=["grow"], tag="gtd")
        self.add("scalar", lambda e: e.activation(
            G[0:1, 18:19], G[0:1, 18:19], AF.Exp),
            r=["grow"], w=["grow"], tag="gtexp")
        self.add("vector", lambda e: e.tensor_scalar_add(
            G[0:1, 19:20], G[0:1, 18:19], 1.0),
            r=["grow"], w=["grow"], tag="gtden")
        self.add("vector", lambda e: e.reciprocal(G[0:1, 19:20], G[0:1, 19:20]),
                 r=["grow"], w=["grow"], tag="gtrec")
        self.add("vector", lambda e: e.tensor_scalar_mul(
            G[0:1, 4:8], G[0:1, 4:8], G[0:1, 19:20]),
            r=["grow"], w=["grow"], tag="gtw1")
        self.add("vector", lambda e: e.tensor_mul(
            G[0:1, 18:19], G[0:1, 18:19], G[0:1, 19:20]),
            r=["grow"], w=["grow"], tag="gtw2a")
        self.add("vector", lambda e: e.tensor_scalar_mul(
            G[0:1, 36:40], G[0:1, 36:40], G[0:1, 18:19]),
            r=["grow"], w=["grow"], tag="gtw2b")
        self.add("vector", lambda e: e.tensor_add(
            G[0:1, 8:12], G[0:1, 4:8], G[0:1, 36:40]),
            r=["grow"], w=["grow", "w4row"], tag="gtw4")
        self.add("tensor", lambda e: e.matmul(
            self.pb[0][0:128, 0:4], self.c_ones[0:1, :], self.grow[0:1, 8:12],
            start=True, stop=True, skip_group_check=True),
            r=["w4row", "c_ones"], w=[self.PB[0]], tag="gtbc")
        self.add("vector", lambda e: e.tensor_copy(
            self.gwb[:, 0:4], self.pb[0][0:128, 0:4]),
            r=[self.PB[0]], w=["gwb"], tag="gtbcE")
        self.add("tensor", lambda e: e.matmul(
            self.pb[0][0:4, 0:1], self.grow[0:1, 8:12], self.one1f[0:1, 0:1],
            start=True, stop=True, skip_group_check=True),
            r=["w4row", "one1f"], w=[self.PB[0]], tag="gtc4")
        self.add("vector", lambda e: e.tensor_copy(
            self.gw4[0:4, 0:1], self.pb[0][0:4, 0:1]),
            r=[self.PB[0]], w=["gw4"], tag="gtc4E")
        self.add("vector", lambda e: e.tensor_scalar_mul(
            self.bc_col[:, 0:24], self.bqkvc[:, 0:24], self.gwb[:, 0:1]),
            r=["bqkvc", "gwb"], w=["bc_col"], tag="gtbb0")
        for e4 in range(1, 4):
            self.add("vector", lambda e, e4=e4: e.scalar_tensor_tensor(
                self.bc_col[:, 0:24], self.bqkvc[:, e4 * 24:(e4 + 1) * 24],
                self.gwb[:, e4:e4 + 1], self.bc_col[:, 0:24],
                ALU.mult, ALU.add),
                r=["bqkvc", "gwb", "bc_col"], w=["bc_col"], tag=f"gtbb{e4}")

    def combine_wc_steps(self):
        steps = []
        for pi, wn in enumerate(("EWq", "EWk", "EWv")):
            for cc in range(NC):
                steps.append(lambda pi=pi, wn=wn, cc=cc: self._wc_one(pi, wn, cc))
        return steps

    def _wc_one(self, pi, wn, cc):
        if True:
            if True:
                for e4 in range(4):
                    stg = self.tf32 if e4 % 2 == 0 else self.wstg
                    stgk = "tf32" if e4 % 2 == 0 else "wstg"
                    self.add("gpsimd", lambda e, wn=wn, cc=cc, e4=e4, stg=stg:
                             e.dma_start(
                        stg[:, :], self.W[wn][e4, cc * 128:(cc + 1) * 128, :]),
                        w=[stgk], dma=True, tag=f"wc{pi}_{cc}_{e4}")
                    sbk = [0, 1] if e4 % 2 == 0 else [2, 3]
                    obk = [2, 3] if e4 % 2 == 0 else [0, 1]
                    for hf in range(2):
                        if e4 == 0:
                            self.add("vector", lambda e, stg=stg, hf=hf, o=obk[hf]:
                                     e.tensor_scalar_mul(
                                self.pb[o][:, :],
                                stg[:, hf * 512:(hf + 1) * 512],
                                self.gwb[:, 0:1]),
                                r=[stgk, "gwb"], w=[self.PB[obk[hf]]],
                                tag=f"wcA{pi}_{cc}_{hf}")
                        else:
                            self.add("vector", lambda e, stg=stg, hf=hf, e4=e4,
                                     i=sbk[hf] ^ 2 if False else (obk[hf] ^ 2),
                                     o=obk[hf]: e.scalar_tensor_tensor(
                                self.pb[o][:, :],
                                stg[:, hf * 512:(hf + 1) * 512],
                                self.gwb[:, e4:e4 + 1],
                                self.pb[o ^ 2][:, :], ALU.mult, ALU.add),
                                r=[stgk, "gwb", self.PB[obk[hf] ^ 2]],
                                w=[self.PB[obk[hf]]],
                                tag=f"wcB{pi}_{cc}_{hf}_{e4}")
                for hf in range(2):
                    self.add("vector", lambda e, hf=hf: e.tensor_copy(
                        self.qtmp[:, 0:512] if hf == 0 else self.t1b[:, 0:512],
                        self.pb[hf][:, :]),
                        r=[self.PB[hf]], w=["qtmp" if hf == 0 else "t1b"],
                        tag=f"wcC{pi}_{cc}_{hf}")
                self.add("sync", lambda e, pi=pi, cc=cc: e.dma_start(
                    self.scr_wc[pi, cc * 128:(cc + 1) * 128, 0:512],
                    self.qtmp[:, 0:512]),
                    r=["qtmp"], w=["scr_wc"], dma=True, tag=f"wcD{pi}_{cc}")
                self.add("sync", lambda e, pi=pi, cc=cc: e.dma_start(
                    self.scr_wc[pi, cc * 128:(cc + 1) * 128, 512:1024],
                    self.t1b[:, 0:512]),
                    r=["t1b"], w=["scr_wc"], dma=True, tag=f"wcE{pi}_{cc}")

    def cast_qkv(self):
        for g in range(NC):
            slab = self.wsA if g % 2 == 0 else self.wsB
            skey = "wsA" if g % 2 == 0 else "wsB"
            self.add("gpsimd", lambda e, g=g, slab=slab: e.dma_start(
                slab[:, 0:3072],
                self.W["wqkv"][g * 128:(g + 1) * 128, :]),
                w=[skey], dma=True, tag=f"qc{g}")
            self.add("sync", lambda e, g=g, slab=slab: e.dma_start(
                self.scr_qkv[g * 128:(g + 1) * 128, :], slab[:, 0:3072]),
                r=[skey], w=["scr_qkv"], dma=True, tag=f"qcs{g}")

    def attn_l1(self, r, wc_steps=()):
        self.a1_tail = None
        wc_steps = list(wc_steps)
        for w in range(4):
            for _ in range(6):
                if wc_steps:
                    wc_steps.pop(0)()
            for s4 in range(4):
                t16 = w * 4 + s4
                self.add("tensor", lambda e, t16=t16: e.matmul(
                    self.pb[2][:, 0:128], self.c_ones[0:1, :],
                    self.rs_row[0:1, t16 * 128:(t16 + 1) * 128],
                    start=True, stop=True, skip_group_check=True),
                    r=[f"rs_row_{t16}", "c_ones"], w=[self.PB[2]],
                    tag=f"abB{w}_{s4}")
                self.add("scalar", lambda e: e.copy(
                    self.rbc[:, 0:128], self.pb[2][:, 0:128]),
                    r=[self.PB[2]], w=["rbcb"], tag=f"abE{w}_{s4}")
                for c in range(NC):
                    self.add("vector", lambda e, c=c, s4=s4, t16=t16:
                             e.scalar_tensor_tensor(
                        self.basep[:, c * 512 + s4 * 128:c * 512 + (s4 + 1) * 128],
                        self.xnT[:, c * T + t16 * 128:c * T + (t16 + 1) * 128],
                        self.g1a[:, c:c + 1], self.rbc[:, 0:128],
                        ALU.mult, ALU.mult),
                        r=["rbcb", f"xnT{c}", "g1a"], w=["shb"],
                        tag=f"abS{w}_{s4}_{c}")
            for pair in range(NPAIR):
                slab = self.wsA if pair % 2 == 0 else self.wsB
                skey = "wsA" if pair % 2 == 0 else "wsB"
                if w == 0 and pair == 0:
                    self.load_l1w(0, 0)
                if pair + 1 < NPAIR:
                    self.load_l1w(w, pair + 1)
                elif w + 1 < 4:
                    self.load_l1w(w + 1, 0)
                self.pair_qkv(pair, slab, skey, rope=False,
                              mov_key_fn=lambda c: "shb",
                              mov_ap_fn=lambda c, lo, hi:
                                  self.basep[:, c * 512 + lo:c * 512 + hi],
                              tq_blocks=1, v_chunks=4, tagp=f"aw{w}")
                self.a1_tail = self.pair_attention(
                    pair, "win", window=w, prev_tail=self.a1_tail,
                    blk0=w * 8 + pair)
        while wc_steps:
            wc_steps.pop(0)()
        if self.a1_tail is not None:
            self.a1_tail()
            self.a1_tail = None
        self.merge_attn(1, "wout")

    def load_l1w(self, w, pair):
        slab = self.wsA if pair % 2 == 0 else self.wsB
        skey = "wsA" if pair % 2 == 0 else "wsB"
        for i in range(3):
            self.load_slab(slab[:, i * 1024:(i + 1) * 1024], skey,
                           self.scr_qkv, (0, D),
                           (i * D + pair * 128, i * D + (pair + 1) * 128),
                           queue="sync", tag=f"l1w{w}_{pair}_{i}")

    def load_fw(self, pair):
        slab = self.wsA if pair % 2 == 0 else self.wsB
        skey = "wsA" if pair % 2 == 0 else "wsB"
        for i in range(3):
            self.load_slab(slab[:, i * 1024:(i + 1) * 1024], skey,
                           self.scr_wc.rearrange("a b c -> (a b) c"),
                           (i * D, (i + 1) * D),
                           (pair * 128, (pair + 1) * 128),
                           queue="sync", tag=f"fw{pair}_{i}")

    def ffn_l1(self):
        self.load_fw(0)
        f1_tail = None
        for pair in range(NPAIR):
            slab = self.wsA if pair % 2 == 0 else self.wsB
            skey = "wsA" if pair % 2 == 0 else "wsB"
            if pair + 1 < NPAIR:
                self.load_fw(pair + 1)
            bias = lambda which, pair=pair: (
                self.bc_col[:, which * 8 + pair:which * 8 + pair + 1],
                ["bc_col"])
            vbias = lambda hh, pair=pair: (
                self.bc_col[:, 16 + pair:17 + pair], ["bc_col"])
            self.pair_qkv(pair, slab, skey, rope=False,
                          mov_key_fn=lambda c: f"xnT{c}",
                          mov_ap_fn=lambda c, lo, hi:
                              self.xnT[:, c * T + lo:c * T + hi],
                          bias=bias, tagp="ff")
            f1_tail = self.pair_attention(pair, "full", vbias=vbias,
                                          prev_tail=f1_tail, blk0=pair * 8)
        if f1_tail is not None:
            f1_tail()
        self.merge_attn(1, "WoF", bias_ct=self.boc, bias_ck="boc")


def build_program(meta, dbg_names=(), phases="all"):
    nc = bass.Bass("TRN2", target_bir_lowering=False)
    es = ExitStack()
    prog = Prog()
    bd = Build(nc, es, prog, meta, dbg_names)
    bd.load_consts()
    for r in range(2):
        bd.load_x(r)
        bd.ln_phase(r, 0)
        if phases == "ln":
            bd.tap("xnT", bd.xnT[:, :], [f"xnT{c}" for c in range(NC)])
            bd.tap("sB", bd.st["sB"][:, :], ["sB"])
            bd.tap("rs", bd.st["rs"][:, :], ["rs"])
            bd.tap("p1", bd.st["p1"][:, :], [f"p1_{t}" for t in range(NT)])
            bd.tap("p2", bd.st["p2"][:, :], [f"p2_{t}" for t in range(NT)])
            bd.tap("ssx", bd.st["ss"][:, :], [f"ss_{t}" for t in range(NT)])
            break
        steps = bd.routing_steps()
        bd.attn_l0(steps)
        if phases == "attn0":
            bd.tap("anorm", bd.anorm[:, :],
                   [k for c in range(NC) for k in bd.an_keys(c * T, (c + 1) * T)])
            bd.tap("maskB", bd.st["m"][:, :], ["maskB"])
            bd.tap("scores", bd.st["sc"][:, :], ["sc"])
            break
        bd.merge_attn(0, "WoT")
        bd.ffn_l0(r)
        if phases == "l0":
            bd.tap("x_l0", bd.x_sb[:, :], [f"x{t}" for t in range(NT)])
            break
        bd.ln_phase(r, 1)
        if r == 0:
            bd.cast_qkv()
        bd.gate_l1()
        bd.attn_l1(r, bd.combine_wc_steps())
        if phases == "attn1":
            bd.tap("x_a1", bd.x_sb[:, :], [f"x{t}" for t in range(NT)])
            break
        bd.ffn_l1()
        bd.store_x(r)
        if phases != "all":
            break
    prog.emit(nc, es)
    es.close()
    return nc, bd


def kernel(**inputs):
    consts = host_constants()
    meta = {}
    meta.update(stage_weights(inputs))
    meta.update(consts)
    nc, bd = build_program(meta)
    x = np.ascontiguousarray(np.asarray(inputs["L0_attn_x"], np.float32))
    in_maps = []
    for c in range(NCORES):
        m = {"hb_" + k: v for k, v in meta.items()}
        m["x_in"] = x[2 * c:2 * c + 2]
        in_maps.append(m)
    res = run_bass_kernel_spmd(nc, in_maps, core_ids=list(range(NCORES)))
    out = np.concatenate([r["x_out"] for r in res.results], axis=0)
    return out.reshape(16, 2048, 1024)
